# version 02
# baseline (speedup 1.0000x reference)
import math
import numpy as np
import concourse.bass as bass
import concourse.mybir as mybir
from concourse.bass_utils import run_bass_kernel_spmd

F32 = mybir.dt.float32
AX = mybir.AxisListType.X
OP = mybir.AluOpType
AF = mybir.ActivationFunctionType

LN_EPS = 1e-5
CLIP = 8.0
MASKMUL = -1e27
ARENA_COLS = 45056


def alibi_slopes(n_heads, bias_max=8.0):
    next_pow2 = 2 ** int(np.ceil(np.log2(n_heads)))
    m = np.arange(1, next_pow2 + 1, dtype=np.float32) * (bias_max / next_pow2)
    slopes = 1.0 / np.power(2.0, m)
    if next_pow2 != n_heads:
        slopes = np.concatenate([slopes[1::2], slopes[::2]])[:n_heads]
    return slopes.astype(np.float32)


class Cfg:
    def __init__(self, D=4096, S=4096, B=2, H=32, NE=16, NR=64):
        self.D, self.S, self.B, self.H = D, S, B, H
        self.DH = 128
        assert H * self.DH == D
        self.NC = 8
        self.R4 = self.NC // B
        self.TOK = B * S // self.NC
        self.NT = self.TOK // 128
        self.KD = D // 128
        self.CH = 512
        self.CHT = min(512, self.TOK)
        self.NTH = self.TOK // self.CHT
        self.TPT = self.CHT // 128
        self.NQC = 3 * D // self.CH
        self.NDC = D // self.CH
        self.NBC = D // self.CH
        self.NE, self.NR = NE, NR
        self.ER = NE * NR
        self.NERC = self.ER // 128
        self.SC = S // self.CH
        self.KB = S // 128
        self.HQ = 2
        self.NHG = H // self.HQ


class Sem:
    def __init__(self, nc, name):
        self.h = nc.alloc_semaphore(name=name)
        self.n = 0


def build_program(c):
    nc = bass.Bass(trn_type="TRN2", target_bir_lowering=False)

    xsh = nc.dram_tensor("xsh", [c.TOK, c.D], F32, kind="ExternalInput").ap()
    qpos = nc.dram_tensor("qpos", [c.TOK, 1], F32, kind="ExternalInput").ap()
    wqkvT = nc.dram_tensor("wqkvT", [c.D, 3 * c.D], F32, kind="ExternalInput").ap()
    woutT = nc.dram_tensor("woutT", [c.D, c.D], F32, kind="ExternalInput").ap()
    basewT = nc.dram_tensor("basewT", [c.D, c.D], F32, kind="ExternalInput").ap()
    baseb = nc.dram_tensor("baseb", [1, c.D], F32, kind="ExternalInput").ap()
    aT = nc.dram_tensor("aT", [c.D, c.NR], F32, kind="ExternalInput").ap()
    routerT = nc.dram_tensor("routerT", [c.D, c.NE], F32, kind="ExternalInput").ap()
    bflat = nc.dram_tensor("bflat", [c.ER, c.D], F32, kind="ExternalInput").ap()
    qlnw = nc.dram_tensor("qlnw", [1, c.D], F32, kind="ExternalInput").ap()
    qlnb = nc.dram_tensor("qlnb", [1, c.D], F32, kind="ExternalInput").ap()
    klnw = nc.dram_tensor("klnw", [1, c.D], F32, kind="ExternalInput").ap()
    klnb = nc.dram_tensor("klnb", [1, c.D], F32, kind="ExternalInput").ap()
    y = nc.dram_tensor("y", [c.TOK, c.D], F32, kind="ExternalOutput").ap()

    xnf_s = nc.dram_tensor("xnf_s", [c.D, c.TOK], F32).ap()
    qs_s = nc.dram_tensor("qs_s", [c.TOK, c.D], F32).ap()
    ks_s = nc.dram_tensor("ks_s", [c.TOK, c.D], F32).ap()
    qf_s = nc.dram_tensor("qf_s", [c.D, c.TOK], F32).ap()
    agin_k = nc.dram_tensor("agin_k", [c.D, c.TOK], F32).ap()
    agin_v = nc.dram_tensor("agin_v", [c.H * c.TOK, c.DH], F32).ap()
    agout_k = nc.dram_tensor("agout_k", [c.R4 * c.D, c.TOK], F32).ap()
    agout_v = nc.dram_tensor("agout_v", [c.R4 * c.H * c.TOK, c.DH], F32).ap()
    base_s = nc.dram_tensor("base_s", [c.TOK, c.D], F32).ap()
    g_s = nc.dram_tensor("g_s", [c.TOK, c.D], F32).ap()
    attnf_s = nc.dram_tensor("attnf_s", [c.D, c.TOK], F32).ap()

    ident_d = nc.inline_tensor(np.eye(128, dtype=np.float32), name="identc").ap()
    kpos_d = nc.inline_tensor(
        np.broadcast_to(np.arange(c.S, dtype=np.float32), (128, c.S)).copy(),
        name="kposc").ap()
    sel_np = np.zeros((c.NE, c.NERC * 128), np.float32)
    for _ecc in range(c.NERC):
        for _m in range(128):
            sel_np[(_ecc * 128 + _m) // c.NR, _ecc * 128 + _m] = 1.0
    sel2_d = nc.inline_tensor(sel_np, name="sel2c").ap()
    ones1_d = nc.inline_tensor(np.ones((1, 128), np.float32), name="ones1c").ap()
    slopes = alibi_slopes(c.H)

    psall = nc.alloc_psum_tensor("psall", [128, 8 * 512], F32).ap()

    def bank(i):
        return psall[:, i * 512:(i + 1) * 512]

    arena = nc.alloc_sbuf_tensor("arena", [128, ARENA_COLS], F32).ap()
    amark = [0]

    def alloc(cols):
        ap = arena[:, amark[0]:amark[0] + cols]
        amark[0] += cols
        assert amark[0] <= ARENA_COLS, f"arena overflow {amark[0]}"
        return ap

    identt = alloc(128)
    kpost = alloc(c.S)
    onest = alloc(128)
    selt = alloc(c.NERC * 128)
    basebt = alloc(c.D)
    gqs, gqb, gks, gkb = alloc(c.H), alloc(c.H), alloc(c.H), alloc(c.H)
    qpost = alloc(c.NT)
    negqp = alloc(c.NT)
    sm = alloc(16 * 36)
    epst = alloc(16)

    def smx(i):
        return sm[:, i * 16:i * 16 + 1]
    PERSIST0 = amark[0]
    logits = alloc(c.NT * c.NE)
    rh2 = alloc(c.TOK)
    combf = alloc(c.TOK)
    atc = alloc(c.KD * c.NR)
    rtc = alloc(c.KD * c.NE)

    def S_(nm):
        return Sem(nc, nm)

    def slots(nm, n):
        return [S_(f"{nm}{i}") for i in range(n)]

    s_sq, s_vst, s_sct, s_xn = S_("sq"), S_("vst"), S_("sct"), S_("xn")
    s_tr0, s_ev0, s_wfr, s_mm1, s_ev1 = (S_(n) for n in ("tr0", "ev0", "wfr", "mm1", "ev1"))
    s_sq2, s_vst2, s_sct2, s_qn, s_tr2, s_ev2 = (S_(n) for n in
        ("sq2", "vst2", "sct2", "qn", "tr2", "ev2"))
    s_cc, s_ffv, s_ffs, s_trc, s_cbm, s_cbv = (S_(n) for n in
        ("cc", "ffv", "ffs", "trc", "cbm", "cbv"))
    s_bfr, s_dmm, s_dev, s_sil = (S_(n) for n in ("bfr", "dmm", "dev", "sil"))
    s_dbs, s_dbv, s_smm, s_sev, s_mx, s_e1, s_rs, s_e2 = (S_(n) for n in
        ("dbs", "dbv", "smm", "sev", "mx", "e1", "rs", "e2"))
    s_ptm, s_ptv, s_pv, s_aev, s_wmm, s_wv1 = (S_(n) for n in
        ("ptm", "ptv", "pv", "aev", "wmm", "wv1"))
    s_ini = S_("ini")
    ldx = slots("ldx", 2)
    st0 = slots("st0", 2)
    s_ldxn = S_("ldxn")
    s_ldpre = S_("ldpre")
    ldw = slots("ldw", 2)
    st1 = slots("st1", 8)
    lqk = slots("lqk", 2)
    st2 = slots("st2", 8)
    ldb = slots("ldb", 2)
    bst = slots("bst", 2)
    st3 = slots("st3", 2)
    s_dup = S_("dup")
    s_kv = S_("kv")
    qfl = slots("qfl", 2)
    ast = slots("ast", 2)
    s_lat = S_("lat")
    gx = slots("gx", 2)
    yst = slots("yst", 2)

    ENGS = ("sync", "gpsimd", "tensor", "vector", "scalar")
    ops = {e: [] for e in ENGS}

    def em(eng, fn):
        ops[eng].append(fn)

    def run_phase():
        nonlocal ops
        with nc.Block() as b:
            for e in ENGS:
                if ops[e]:
                    getattr(b, e)(lambda eng, fns=tuple(ops[e]): [f(eng) for f in fns])
        nc.all_engine_barrier()
        ops = {e: [] for e in ENGS}

    def W(*pairs):
        frozen = tuple((s.h, v) for (s, v) in pairs if v > 0)

        def apply(e):
            for h, v in frozen:
                e.wait_ge(h, v)
        return apply

    def DMA(eng, out, in_, sem, *waits, slow=False):
        w = W(*waits)

        def f(e, out=out, in_=in_):
            w(e)
            e.dma_start(out=out, in_=in_,
                        allow_slow_non_contiguous=slow).then_inc(sem.h, 16)
        em(eng, f)
        sem.n += 16
        return sem.n

    def quiesce(eng, sems):
        pairs = tuple((sx.h, sx.n) for sx in sems if sx.n > 0)

        def f(e):
            for h, v in pairs:
                e.wait_ge(h, v)
        em(eng, f)

    DMA("sync", identt, ident_d, s_ini)
    DMA("sync", kpost, kpos_d, s_ini)
    DMA("sync", selt[0:c.NE, :], sel2_d, s_ini)
    DMA("sync", onest[0:1, :], ones1_d, s_ini)
    DMA("sync", basebt[0:1, :], baseb, s_ini)
    for dst, src in ((gqs, qlnw), (gqb, qlnb), (gks, klnw), (gkb, klnb)):
        DMA("sync", dst, src[0, :].rearrange("(h d) -> d h", d=128), s_ini, slow=True)
    DMA("sync", qpost, qpos[:, 0].rearrange("(t p) -> p t", p=128), s_ini, slow=True)

    def f_initv(e, w=W((s_ini, s_ini.n))):
        w(e)
        isq = 1.0 / math.sqrt(c.DH)
        e.tensor_scalar(gqs, gqs, isq, None, OP.mult)
        e.drain()
        e.tensor_scalar(gqb, gqb, isq, None, OP.mult)
        e.drain()
        e.tensor_scalar(negqp, qpost, -1.0, None, OP.mult)
        e.drain()
        e.memset(epst[:, 0:1], LN_EPS)
    em("vector", f_initv)
    quiesce("sync", [s_ini])
    run_phase()

    am0 = amark[0]
    xa = [alloc(c.D), alloc(c.D)]
    xnb = [alloc(c.D), alloc(c.D)]
    sqs = alloc(c.D)
    stg0 = [alloc(512), alloc(512)]
    NG0 = c.KD // 4

    for i in range(c.NT):
        sl = i % 2
        DMA("sync", xa[sl], xsh[i * 128:(i + 1) * 128, :], ldx[sl],
            (s_xn, max(0, s_xn.n - 1)), (s_sq, max(0, s_sq.n - 1)))

        def f_sc(e, sl=sl, w=W((ldx[sl], ldx[sl].n))):
            w(e)
            e.activation(sqs, xa[sl], AF.Square,
                         accum_out=smx(2 + sl)).then_inc(s_sq.h, 1)
        em("scalar", f_sc)
        s_sq.n += 1

        def f_v1(e, sl=sl, w=W((ldx[sl], ldx[sl].n), (s_sq, s_sq.n))):
            w(e)
            e.reduce_sum(smx(0 + sl), xa[sl], axis=AX)
            e.drain()
            e.tensor_scalar(smx(4 + sl), smx(0 + sl), 1.0 / c.D, None, OP.mult)
            e.tensor_scalar(smx(6 + sl), smx(2 + sl), 1.0 / c.D, None, OP.mult)
            e.drain()
            e.tensor_tensor(smx(8 + sl), smx(4 + sl), smx(4 + sl), OP.mult)
            e.drain()
            e.tensor_tensor(smx(6 + sl), smx(6 + sl), smx(8 + sl),
                            OP.subtract).then_inc(s_vst.h, 1)
        em("vector", f_v1)
        s_vst.n += 1

        def f_sc2(e, sl=sl, w=W((s_vst, s_vst.n))):
            w(e)
            e.activation(smx(10 + sl), smx(6 + sl), AF.Sqrt,
                         bias=epst[:, 0:1]).then_inc(s_sct.h, 1)
        em("scalar", f_sc2)
        s_sct.n += 1

        def f_v2(e, sl=sl, w=W((s_sct, s_sct.n), (s_tr0, max(0, (i - 1) * NG0)))):
            w(e)
            e.reciprocal(smx(12 + sl), smx(10 + sl))
            e.drain()
            e.tensor_scalar(xnb[sl], xa[sl], smx(4 + sl), smx(12 + sl),
                            OP.subtract, OP.mult).then_inc(s_xn.h, 1)
        em("vector", f_v2)
        s_xn.n += 1

        for g in range(NG0):
            def f_tr(e, sl=sl, g=g, w=W((s_xn, s_xn.n), (s_ev0, max(0, s_ev0.n - 1)))):
                w(e)
                for q in range(4):
                    kd = g * 4 + q
                    ins = e.transpose(bank(g % 2)[:, q * 128:(q + 1) * 128],
                                      xnb[sl][:, kd * 128:(kd + 1) * 128], identt)
                ins.then_inc(s_tr0.h, 1)
            em("tensor", f_tr)
            s_tr0.n += 1

            def f_ev(e, g=g, w=W((s_tr0, s_tr0.n), (st0[g % 2], st0[g % 2].n))):
                w(e)
                e.tensor_copy(stg0[g % 2], bank(g % 2)).then_inc(s_ev0.h, 1)
            em("vector", f_ev)
            s_ev0.n += 1

            for q in range(4):
                kd = g * 4 + q
                DMA("gpsimd", xnf_s[kd * 128:(kd + 1) * 128, i * 128:(i + 1) * 128],
                    stg0[g % 2][:, q * 128:(q + 1) * 128], st0[g % 2],
                    (s_ev0, s_ev0.n) if q == 0 else (s_ev0, 0))
    quiesce("sync", ldx)
    quiesce("gpsimd", st0)
    run_phase()
    amark[0] = am0

    am1 = amark[0]
    xnf = alloc(c.KD * c.CHT)
    wsb = [alloc(512), alloc(512)]
    stg1 = [alloc(512) for _ in range(8)]
    NSTG = 8
    TPT = c.TPT

    chunks = ([("qkv", i) for i in range(c.NQC)] + [("base", i) for i in range(c.NBC)]
              + [("arh", 0), ("rtr", 0)])
    NCHK = len(chunks)
    wtile = 0
    mm1_at_th_end = []
    wfr_at_th_end = []
    st0_finals = [(st0[0], st0[0].n), (st0[1], st0[1].n)]

    for kd in range(c.KD):
        DMA("sync", atc[:, kd * c.NR:(kd + 1) * c.NR], aT[kd * 128:(kd + 1) * 128, :], s_ldpre)
        DMA("sync", rtc[:, kd * c.NE:(kd + 1) * c.NE], routerT[kd * 128:(kd + 1) * 128, :],
            s_ldpre)
    ldpre_fin = s_ldpre.n

    suse = 0
    for th in range(c.NTH):
        for kd in range(c.KD):
            extra = (st0_finals + [(s_mm1, mm1_at_th_end[th - 1] if th > 0 else 0),
                                   (s_wfr, wfr_at_th_end[th - 1] if th > 0 else 0)]
                     if kd == 0 else [])
            DMA("gpsimd", xnf[:, kd * c.CHT:(kd + 1) * c.CHT],
                xnf_s[kd * 128:(kd + 1) * 128, th * c.CHT:(th + 1) * c.CHT], s_ldxn, *extra)
        ldxn_here = s_ldxn.n

        for ci, (kind, ec) in enumerate(chunks):
            cgi = th * NCHK + ci
            par = (cgi % 2) * 4
            ev_guard = (cgi - 2 + 1) * TPT if cgi >= 2 else 0

            if kind in ("qkv", "base"):
                src = wqkvT if kind == "qkv" else basewT
                for kd in range(c.KD):
                    sl_w = wtile % 2
                    DMA("sync", wsb[sl_w],
                        src[kd * 128:(kd + 1) * 128, ec * 512:ec * 512 + 512], ldw[sl_w],
                        (s_wfr, max(0, wtile - 1)))
                    wtile += 1

                    def f_mm(e, kd=kd, par=par, sl_w=sl_w, kind=kind,
                             w=W((ldw[sl_w], ldw[sl_w].n), (s_ldxn, ldxn_here),
                                 (s_ev1, ev_guard if kd == 0 else 0))):
                        w(e)
                        stop = (kd == c.KD - 1) and kind == "qkv"
                        for tt in range(TPT):
                            ins = e.matmul(
                                bank(par + tt),
                                xnf[:, kd * c.CHT + tt * 128:kd * c.CHT + tt * 128 + 128],
                                wsb[sl_w], start=(kd == 0), stop=stop)
                        ins.then_inc(s_wfr.h, 1)
                    em("tensor", f_mm)
                    s_wfr.n += 1

                if kind == "base":
                    def f_bias(e, par=par, ec=ec):
                        for tt in range(TPT):
                            ins = e.matmul(bank(par + tt), onest[0:1, 0:128],
                                           basebt[0:1, ec * 512:ec * 512 + 512],
                                           start=False, stop=True)
                        ins.then_inc(s_mm1.h, 1)
                    em("tensor", f_bias)
                    s_mm1.n += 1
                done_w = (s_mm1, s_mm1.n) if kind == "base" else (s_wfr, s_wfr.n)

                for tt in range(TPT):
                    sl1 = suse % NSTG

                    def f_ev1(e, par=par, tt=tt, kind=kind, sl1=sl1,
                              w=W(done_w, (st1[sl1], st1[sl1].n))):
                        w(e)
                        if kind == "qkv":
                            e.tensor_scalar(stg1[sl1], bank(par + tt), CLIP, -CLIP,
                                            OP.min, OP.max).then_inc(s_ev1.h, 1)
                        else:
                            e.tensor_copy(stg1[sl1], bank(par + tt)).then_inc(s_ev1.h, 1)
                    em("vector", f_ev1)
                    s_ev1.n += 1

                    trow = th * c.CHT + tt * 128
                    if kind == "qkv":
                        if ec < c.NDC:
                            DMA("gpsimd", qs_s[trow:trow + 128, ec * 512:(ec + 1) * 512],
                                stg1[sl1], st1[sl1], (s_ev1, s_ev1.n))
                        elif ec < 2 * c.NDC:
                            DMA("gpsimd",
                                ks_s[trow:trow + 128, (ec - c.NDC) * 512:(ec - c.NDC + 1) * 512],
                                stg1[sl1], st1[sl1], (s_ev1, s_ev1.n))
                        else:
                            for hb in range(4):
                                h = (ec - 2 * c.NDC) * 4 + hb
                                DMA("gpsimd",
                                    agin_v[h * c.TOK + trow:h * c.TOK + trow + 128, :],
                                    stg1[sl1][:, hb * 128:(hb + 1) * 128], st1[sl1],
                                    (s_ev1, s_ev1.n) if hb == 0 else (s_ev1, 0))
                    else:
                        DMA("gpsimd", base_s[trow:trow + 128, ec * 512:(ec + 1) * 512],
                            stg1[sl1], st1[sl1], (s_ev1, s_ev1.n))
                    suse += 1

            elif kind == "arh":
                def f_arh(e, par=par,
                          w=W((s_ldpre, ldpre_fin), (s_ldxn, ldxn_here), (s_ev1, ev_guard))):
                    w(e)
                    for kd in range(c.KD):
                        ins = e.matmul(bank(par)[0:c.NR, 0:c.CHT],
                                       atc[:, kd * c.NR:(kd + 1) * c.NR],
                                       xnf[:, kd * c.CHT:(kd + 1) * c.CHT],
                                       start=(kd == 0), stop=(kd == c.KD - 1))
                    ins.then_inc(s_mm1.h, 1)
                em("tensor", f_arh)
                s_mm1.n += 1

                def f_evarh(e, th=th, par=par, w=W((s_mm1, s_mm1.n))):
                    w(e)
                    e.tensor_copy(rh2[0:c.NR, th * c.CHT:(th + 1) * c.CHT],
                                  bank(par)[0:c.NR, 0:c.CHT]).then_inc(s_ev1.h, TPT)
                em("vector", f_evarh)
                s_ev1.n += TPT
                suse += TPT

            else:
                def f_rtr(e, par=par,
                          w=W((s_ldpre, ldpre_fin), (s_ldxn, ldxn_here), (s_ev1, ev_guard))):
                    w(e)
                    for tt in range(TPT):
                        for kd in range(c.KD):
                            ins = e.matmul(
                                bank(par + tt)[:, 0:c.NE],
                                xnf[:, kd * c.CHT + tt * 128:kd * c.CHT + tt * 128 + 128],
                                rtc[:, kd * c.NE:(kd + 1) * c.NE],
                                start=(kd == 0), stop=(kd == c.KD - 1))
                    ins.then_inc(s_mm1.h, 1)
                em("tensor", f_rtr)
                s_mm1.n += 1

                def f_evrtr(e, th=th, par=par, w=W((s_mm1, s_mm1.n))):
                    w(e)
                    for tt in range(TPT):
                        g = th * TPT + tt
                        ins = e.tensor_copy(logits[:, g * c.NE:(g + 1) * c.NE],
                                            bank(par + tt)[:, 0:c.NE])
                    ins.then_inc(s_ev1.h, TPT)
                em("vector", f_evrtr)
                s_ev1.n += TPT
                suse += TPT
        mm1_at_th_end.append(s_mm1.n)
        wfr_at_th_end.append(s_wfr.n)
    quiesce("sync", [s_ldpre] + ldw)
    quiesce("gpsimd", [s_ldxn] + st1)
    run_phase()
    amark[0] = am1
    st1_finals = [(s, s.n) for s in st1]

    am2 = amark[0]
    qa = [alloc(c.D), alloc(c.D)]
    ka = [alloc(c.D), alloc(c.D)]
    qn = qa
    kn = ka
    sq2 = alloc(c.D)
    stg2 = [alloc(128) for _ in range(8)]
    suse2 = 0

    for i in range(c.NT):
        sl = i % 2
        extra = (st1_finals if i == 0 else []) + [(s_qn, max(0, s_qn.n - 2)),
                                                   (s_tr2, max(0, (i - 1) * 2 * c.H))]
        DMA("gpsimd", qa[sl], qs_s[i * 128:(i + 1) * 128, :], lqk[sl], *extra)
        DMA("gpsimd", ka[sl], ks_s[i * 128:(i + 1) * 128, :], lqk[sl])
        for which, buf, nbuf, gw, gb in (("q", qa, qn, gqs, gqb), ("k", ka, kn, gks, gkb)):
            o = 0 if which == "q" else 16

            def f_sc(e, sl=sl, buf=buf, o=o, w=W((lqk[sl], lqk[sl].n))):
                w(e)
                e.activation(sq2, buf[sl], AF.Square,
                             accum_out=smx(o + 2 + sl)).then_inc(s_sq2.h, 1)
            em("scalar", f_sc)
            s_sq2.n += 1

            def f_v1(e, sl=sl, buf=buf, o=o, w=W((lqk[sl], lqk[sl].n), (s_sq2, s_sq2.n))):
                w(e)
                e.reduce_sum(smx(o + 0 + sl), buf[sl], axis=AX)
                e.drain()
                e.tensor_scalar(smx(o + 4 + sl), smx(o + 0 + sl),
                                1.0 / c.D, None, OP.mult)
                e.tensor_scalar(smx(o + 6 + sl), smx(o + 2 + sl),
                                1.0 / c.D, None, OP.mult)
                e.drain()
                e.tensor_tensor(smx(o + 8 + sl), smx(o + 4 + sl),
                                smx(o + 4 + sl), OP.mult)
                e.drain()
                e.tensor_tensor(smx(o + 6 + sl), smx(o + 6 + sl),
                                smx(o + 8 + sl), OP.subtract).then_inc(s_vst2.h, 1)
            em("vector", f_v1)
            s_vst2.n += 1

            def f_sc2(e, sl=sl, o=o, w=W((s_vst2, s_vst2.n))):
                w(e)
                e.activation(smx(o + 10 + sl), smx(o + 6 + sl),
                             AF.Sqrt, bias=epst[:, 0:1]).then_inc(s_sct2.h, 1)
            em("scalar", f_sc2)
            s_sct2.n += 1

            def f_v2(e, sl=sl, buf=buf, nbuf=nbuf, o=o,
                     w=W((s_sct2, s_sct2.n), (s_tr2, max(0, (i - 1) * 2 * c.H)))):
                w(e)
                e.reciprocal(smx(o + 12 + sl), smx(o + 10 + sl))
                e.drain()
                e.tensor_scalar(nbuf[sl], buf[sl], smx(o + 4 + sl),
                                smx(o + 12 + sl),
                                OP.subtract, OP.mult).then_inc(s_qn.h, 1)
            em("vector", f_v2)
            s_qn.n += 1

            for h in range(c.H):
                bnk = (h % 4) + (0 if sl == 0 else 4)

                def f_tr(e, sl=sl, nbuf=nbuf, h=h, bnk=bnk,
                         w=W((s_qn, s_qn.n), (s_ev2, max(0, s_ev2.n - 3)))):
                    w(e)
                    e.transpose(bank(bnk)[:, 0:128], nbuf[sl][:, h * 128:(h + 1) * 128],
                                identt).then_inc(s_tr2.h, 1)
                em("tensor", f_tr)
                s_tr2.n += 1

                s2 = suse2 % 8

                def f_ev(e, h=h, bnk=bnk, gw=gw, gb=gb, s2=s2,
                         w=W((s_tr2, s_tr2.n), (st2[s2], st2[s2].n))):
                    w(e)
                    e.tensor_scalar(stg2[s2], bank(bnk)[:, 0:128],
                                    gw[:, h:h + 1], gb[:, h:h + 1],
                                    OP.mult, OP.add).then_inc(s_ev2.h, 1)
                em("vector", f_ev)
                s_ev2.n += 1
                dst = qf_s if which == "q" else agin_k
                DMA("gpsimd", dst[h * 128:(h + 1) * 128, i * 128:(i + 1) * 128],
                    stg2[s2], st2[s2], (s_ev2, s_ev2.n))
                suse2 += 1

    st2_finals = [(s, s.n) for s in st2]

    def f_ag(e, w=W(*(st2_finals + st1_finals))):
        w(e)
        groups = [list(range(g * c.R4, (g + 1) * c.R4)) for g in range(c.B)]
        e.collective_compute("AllGather", OP.bypass, replica_groups=groups,
                             ins=[agin_k.opt()], outs=[agout_k.opt()]).then_inc(s_cc.h, 1)
        e.collective_compute("AllGather", OP.bypass, replica_groups=groups,
                             ins=[agin_v.opt()], outs=[agout_v.opt()]).then_inc(s_cc.h, 1)
    em("gpsimd", f_ag)
    s_cc.n += 2
    quiesce("gpsimd", lqk + st2)
    run_phase()
    amark[0] = am2

    csb = alloc(c.NERC * c.TOK)
    wsb3 = [alloc(512), alloc(512)]
    bstg = [alloc(512), alloc(512)]
    t1st = [alloc(512), alloc(512)]
    gstg = [alloc(512), alloc(512)]
    prbf = alloc(64)
    prb2f = alloc(64)
    prb = prbf[:, 0:c.NE]
    prb2 = prb2f[:, 0:c.NE]

    for g in range(c.NT):
        lg = logits[:, g * c.NE:(g + 1) * c.NE]

        def f_v(e, lg=lg, w=W((s_ffs, s_ffs.n), (s_trc, s_trc.n))):
            w(e)
            e.reduce_max(smx(0), lg, axis=AX)
            e.drain()
            e.tensor_scalar(smx(1), smx(0), -1.0, None, OP.mult).then_inc(s_ffv.h, 1)
        em("vector", f_v)
        s_ffv.n += 1

        def f_s(e, lg=lg, w=W((s_ffv, s_ffv.n))):
            w(e)
            e.activation(prb, lg, AF.Exp, bias=smx(1),
                         accum_out=smx(2)).then_inc(s_ffs.h, 1)
        em("scalar", f_s)
        s_ffs.n += 1

        def f_v2(e, w=W((s_ffs, s_ffs.n))):
            w(e)
            e.reciprocal(smx(3), smx(2))
            e.drain()
            e.tensor_scalar(prb, prb, smx(3), None, OP.mult)
            e.drain()
            e.reduce_max(smx(4), prb, axis=AX)
            e.drain()
            e.tensor_scalar(prb2, prb, smx(4), None, OP.is_ge)
            e.drain()
            e.scalar_tensor_tensor(prb2, prb2, -2.0, prb, OP.mult, OP.add)
            e.drain()
            e.reduce_max(smx(5), prb2, axis=AX)
            e.drain()
            e.tensor_tensor(smx(6), smx(4), smx(5), OP.add)
            e.drain()
            e.tensor_scalar(smx(6), smx(6), 1e-6, None, OP.add)
            e.drain()
            e.reciprocal(smx(7), smx(6))
            e.drain()
            e.tensor_tensor(smx(8), smx(4), smx(7), OP.mult)
            e.tensor_tensor(smx(9), smx(5), smx(7), OP.mult)
            e.tensor_scalar(prb2, prb2, smx(5), None, OP.is_ge)
            e.tensor_scalar(prb, prb, smx(4), None, OP.is_ge)
            e.drain()
            e.tensor_scalar(prb, prb, smx(8), None, OP.mult)
            e.drain()
            e.scalar_tensor_tensor(prb, prb2, smx(9), prb, OP.mult,
                                   OP.add).then_inc(s_ffv.h, 1)
        em("vector", f_v2)
        s_ffv.n += 1

        def f_tr(e, g=g, w=W((s_ffv, s_ffv.n), (s_cbv, max(0, s_cbv.n - 1)))):
            w(e)
            e.transpose(bank(g % 2)[0:c.NE, 0:128], prb, identt).then_inc(s_trc.h, 1)
        em("tensor", f_tr)
        s_trc.n += 1

        def f_ev(e, g=g, w=W((s_trc, s_trc.n))):
            w(e)
            e.tensor_copy(combf[0:c.NE, g * 128:(g + 1) * 128],
                          bank(g % 2)[0:c.NE, 0:128]).then_inc(s_cbv.h, 1)
        em("vector", f_ev)
        s_cbv.n += 1

    def f_dup(e, w=W((s_ev1, s_ev1.n))):
        w(e)
        e.dma_start(out=rh2[c.NR:2 * c.NR, :], in_=rh2[0:c.NR, :]).then_inc(s_dup.h, 16)
    em("gpsimd", f_dup)
    s_dup.n += 16

    cbv_base = s_cbv.n
    cb = 0
    for ecc in range(c.NERC):
        for tch in range(c.NTH):
            def f_cb(e, ecc=ecc, tch=tch, cb=cb,
                     w=W((s_cbv, cbv_base + max(0, cb - 1)), (s_dup, 16))):
                w(e)
                e.matmul(bank(2 + cb % 2)[:, 0:c.CHT],
                         selt[0:c.NE, ecc * 128:(ecc + 1) * 128],
                         combf[0:c.NE, tch * c.CHT:(tch + 1) * c.CHT],
                         start=True, stop=True).then_inc(s_cbm.h, 1)
            em("tensor", f_cb)
            s_cbm.n += 1

            def f_cv(e, ecc=ecc, tch=tch, cb=cb, w=W((s_cbm, s_cbm.n))):
                w(e)
                e.tensor_tensor(
                    csb[:, ecc * c.TOK + tch * c.CHT:ecc * c.TOK + (tch + 1) * c.CHT],
                    rh2[:, tch * c.CHT:(tch + 1) * c.CHT],
                    bank(2 + cb % 2)[:, 0:c.CHT], OP.mult).then_inc(s_cbv.h, 1)
            em("vector", f_cv)
            s_cbv.n += 1
            cb += 1

    btile = 0
    for oc in range(c.NBC):
        for ecc in range(c.NERC):
            sl_b = btile % 2
            DMA("sync", wsb3[sl_b],
                bflat[ecc * 128:(ecc + 1) * 128, oc * 512:(oc + 1) * 512], ldb[sl_b],
                (s_bfr, max(0, btile - 1)))
            btile += 1

            def f_dm(e, ecc=ecc, sl_b=sl_b, oc=oc,
                     w=W((ldb[sl_b], ldb[sl_b].n), (s_cbv, s_cbv.n),
                         (s_dev, oc * c.NT if ecc == 0 else 0))):
                w(e)
                for tt in range(c.NT):
                    ins = e.matmul(bank(tt),
                                   csb[:, ecc * c.TOK + tt * 128:ecc * c.TOK + (tt + 1) * 128],
                                   wsb3[sl_b], start=(ecc == 0), stop=(ecc == c.NERC - 1))
                ins.then_inc(s_bfr.h, 1)
            em("tensor", f_dm)
            s_bfr.n += 1

        for tt in range(c.NT):
            dl = (oc * c.NT + tt) % 2
            extra = st1_finals if (oc == 0 and tt == 0) else []
            DMA("gpsimd", bstg[dl], base_s[tt * 128:(tt + 1) * 128, oc * 512:(oc + 1) * 512],
                bst[dl], (s_dev, max(0, s_dev.n - 1)), *extra)

            def f_dv(e, tt=tt, dl=dl, w=W((s_bfr, s_bfr.n), (bst[dl], bst[dl].n),
                                          (s_sil, max(0, s_sil.n - 1)))):
                w(e)
                e.scalar_tensor_tensor(t1st[dl], bank(tt), 0.5, bstg[dl],
                                       OP.mult, OP.add).then_inc(s_dev.h, 1)
            em("vector", f_dv)
            s_dev.n += 1

            def f_sil(e, dl=dl, w=W((s_dev, s_dev.n), (st3[dl], st3[dl].n))):
                w(e)
                e.activation(gstg[dl], t1st[dl], AF.Silu).then_inc(s_sil.h, 1)
            em("scalar", f_sil)
            s_sil.n += 1
            DMA("gpsimd", g_s[tt * 128:(tt + 1) * 128, oc * 512:(oc + 1) * 512],
                gstg[dl], st3[dl], (s_sil, s_sil.n))
    quiesce("sync", ldb)
    quiesce("gpsimd", bst + st3 + [s_dup])
    run_phase()

    amark[0] = PERSIST0
    kf = alloc(c.HQ * c.S)
    vt = alloc(c.HQ * c.S)
    ddt = alloc(c.S)
    dst_ = alloc(c.S)
    scb = [alloc(c.S), alloc(c.S)]
    ptsb = [alloc(512), alloc(512)]
    qtl = [alloc(128), alloc(128)]
    astg = [alloc(128), alloc(128)]
    NG4 = c.KB // 4
    st2f = list(st2_finals)

    for hg in range(c.NHG):
        kv_first = True
        for hh in range(c.HQ):
            h = hg * c.HQ + hh
            for rp in range(c.R4):
                gw_ = ((s_smm, s_smm.n), (s_pv, s_pv.n), (s_cc, s_cc.n)) if kv_first else ()
                kv_first = False
                DMA("sync", kf[:, hh * c.S + rp * c.TOK: hh * c.S + (rp + 1) * c.TOK],
                    agout_k[rp * c.D + h * 128: rp * c.D + (h + 1) * 128, :], s_kv, *gw_)
                for lb in range(c.TOK // 128):
                    vcol = hh * c.S + rp * c.TOK + lb * 128
                    vrow = (rp * c.H + h) * c.TOK + lb * 128
                    DMA("sync", vt[:, vcol:vcol + 128],
                        agout_v[vrow:vrow + 128, :], s_kv)
        kv_here = s_kv.n

        for qt in range(c.NT):
            def f_db(e, qt=qt, w=W((s_sev, s_sev.n))):
                w(e)
                e.activation(ddt, kpost, AF.Identity, bias=negqp[:, qt:qt + 1])
                e.activation(dst_, kpost, AF.Relu,
                             bias=negqp[:, qt:qt + 1]).then_inc(s_dbs.h, 1)
            em("scalar", f_db)
            s_dbs.n += 1

            def f_dbv(e, w=W((s_dbs, s_dbs.n))):
                w(e)
                e.scalar_tensor_tensor(dst_, dst_, MASKMUL, ddt, OP.mult,
                                       OP.add).then_inc(s_dbv.h, 1)
            em("vector", f_dbv)
            s_dbv.n += 1

            for hh in range(c.HQ):
                h = hg * c.HQ + hh
                extra = st2f if (hg == 0 and qt == 0 and hh == 0) else []
                DMA("sync", qtl[hh], qf_s[h * 128:(h + 1) * 128, qt * 128:(qt + 1) * 128],
                    qfl[hh], (s_smm, max(0, s_smm.n - (c.HQ - 1 - hh) * c.SC)), *extra)

            escnt = {}
            for hh in range(c.HQ):
                h = hg * c.HQ + hh
                for sc in range(c.SC):
                    def f_smm(e, hh=hh, sc=sc,
                              w=W((s_kv, kv_here), (qfl[hh], qfl[hh].n),
                                  (s_sev, max(0, s_sev.n - 1)))):
                        w(e)
                        e.matmul(bank(sc % 2), qtl[hh],
                                 kf[:, hh * c.S + sc * 512: hh * c.S + (sc + 1) * 512],
                                 start=True, stop=True).then_inc(s_smm.h, 1)
                    em("tensor", f_smm)
                    s_smm.n += 1

                    def f_sev(e, hh=hh, sc=sc, slp=float(slopes[h]),
                              w=W((s_smm, s_smm.n), (s_dbv, s_dbv.n),
                                  (s_e2, max(0, s_e2.n - 1)))):
                        w(e)
                        e.scalar_tensor_tensor(scb[hh][:, sc * 512:(sc + 1) * 512],
                                               dst_[:, sc * 512:(sc + 1) * 512], slp,
                                               bank(sc % 2), OP.mult,
                                               OP.add).then_inc(s_sev.h, 1)
                    em("vector", f_sev)
                    s_sev.n += 1

                def f_mx(e, hh=hh, w=W((s_sev, s_sev.n))):
                    w(e)
                    e.drain()
                    e.reduce_max(smx(20 + hh), scb[hh], axis=AX)
                    e.drain()
                    e.tensor_scalar(smx(22 + hh), smx(20 + hh),
                                    -1.0, None, OP.mult).then_inc(s_mx.h, 1)
                em("vector", f_mx)
                s_mx.n += 1

                def f_e1(e, hh=hh, w=W((s_mx, s_mx.n))):
                    w(e)
                    e.activation(scb[hh], scb[hh], AF.Exp, bias=smx(22 + hh),
                                 accum_out=smx(24 + hh)).then_inc(s_e1.h, 1)
                em("scalar", f_e1)
                s_e1.n += 1

                def f_rs(e, hh=hh, w=W((s_e1, s_e1.n))):
                    w(e)
                    e.reciprocal(smx(26 + hh),
                                 smx(24 + hh)).then_inc(s_rs.h, 1)
                em("vector", f_rs)
                s_rs.n += 1

                def f_e2(e, hh=hh, w=W((s_rs, s_rs.n))):
                    w(e)
                    e.activation(scb[hh], scb[hh], AF.Identity,
                                 scale=smx(26 + hh)).then_inc(s_e2.h, 1)
                em("scalar", f_e2)
                s_e2.n += 1
                escnt[hh] = s_e2.n

            for hh in range(c.HQ):
                h = hg * c.HQ + hh
                for gk in range(NG4):
                    def f_pt(e, hh=hh, gk=gk,
                             w=W((s_e2, escnt[hh]), (s_ptv, max(0, s_ptv.n - 1)))):
                        w(e)
                        for q4 in range(4):
                            kb = gk * 4 + q4
                            ins = e.transpose(bank(2 + gk % 2)[:, q4 * 128:(q4 + 1) * 128],
                                              scb[hh][:, kb * 128:(kb + 1) * 128], identt)
                        ins.then_inc(s_ptm.h, 1)
                    em("tensor", f_pt)
                    s_ptm.n += 1

                    def f_ptv(e, gk=gk, w=W((s_ptm, s_ptm.n))):
                        w(e)
                        e.tensor_copy(ptsb[gk % 2], bank(2 + gk % 2)).then_inc(s_ptv.h, 1)
                    em("vector", f_ptv)
                    s_ptv.n += 1

                    def f_pvm(e, hh=hh, gk=gk,
                              w=W((s_ptv, s_ptv.n),
                                  (s_aev, max(0, s_aev.n - 1) if gk == 0 else 0))):
                        w(e)
                        for q4 in range(4):
                            kb = gk * 4 + q4
                            ins = e.matmul(bank(4 + hh)[:, 0:128],
                                           vt[:, hh * c.S + kb * 128: hh * c.S + (kb + 1) * 128],
                                           ptsb[gk % 2][:, q4 * 128:(q4 + 1) * 128],
                                           start=(kb == 0), stop=(kb == c.KB - 1))
                        if gk == NG4 - 1:
                            ins.then_inc(s_pv.h, 1)
                    em("tensor", f_pvm)
                s_pv.n += 1

                def f_aev(e, hh=hh, w=W((s_pv, s_pv.n), (ast[hh], ast[hh].n))):
                    w(e)
                    e.tensor_copy(astg[hh], bank(4 + hh)[:, 0:128]).then_inc(s_aev.h, 1)
                em("vector", f_aev)
                s_aev.n += 1
                DMA("gpsimd", attnf_s[h * 128:(h + 1) * 128, qt * 128:(qt + 1) * 128],
                    astg[hh], ast[hh], (s_aev, s_aev.n))
    quiesce("sync", [s_kv] + qfl)
    quiesce("gpsimd", ast)
    run_phase()
    amark[0] = PERSIST0

    atn = alloc(c.KD * c.CHT)
    wsb5 = [alloc(512), alloc(512)]
    gld5 = [alloc(512), alloc(512)]
    xld5 = [alloc(512), alloc(512)]
    t1s = [alloc(512), alloc(512)]
    ost = [alloc(512), alloc(512)]
    ast_finals = [(s, s.n) for s in ast]
    st3_finals = [(s, s.n) for s in st3]
    TPT = c.TPT
    wfr5_after = []

    for th in range(c.NTH):
        for fc in range(c.KD):
            extra = (ast_finals + [(s_wfr, wfr5_after[th * c.NBC - 1] if th > 0 else 0)]
                     if fc == 0 else [])
            DMA("gpsimd", atn[:, fc * c.CHT:(fc + 1) * c.CHT],
                attnf_s[fc * 128:(fc + 1) * 128, th * c.CHT:(th + 1) * c.CHT], s_lat, *extra)
        lat_here = s_lat.n
        for oc in range(c.NBC):
            cgi5 = th * c.NBC + oc
            par = (cgi5 % 2) * 4
            for fc in range(c.KD):
                sl_w = (cgi5 * c.KD + fc) % 2
                DMA("sync", wsb5[sl_w],
                    woutT[fc * 128:(fc + 1) * 128, oc * 512:(oc + 1) * 512], ldw[sl_w],
                    (s_wfr, max(0, s_wfr.n - 1)))

                def f_wm(e, fc=fc, par=par, sl_w=sl_w,
                         w=W((ldw[sl_w], ldw[sl_w].n), (s_lat, lat_here),
                             (s_wv1, (cgi5 - 2 + 1) * TPT if (cgi5 >= 2 and fc == 0) else 0))):
                    w(e)
                    for tt in range(TPT):
                        ins = e.matmul(bank(par + tt),
                                       atn[:, fc * c.CHT + tt * 128:fc * c.CHT + tt * 128 + 128],
                                       wsb5[sl_w], start=(fc == 0),
                                       stop=(fc == c.KD - 1))
                    ins.then_inc(s_wfr.h, 1)
                em("tensor", f_wm)
                s_wfr.n += 1
            wfr5_after.append(s_wfr.n)

            for tt in range(TPT):
                dl = (cgi5 * TPT + tt) % 2
                trow = th * c.CHT + tt * 128
                extra = st3_finals if (cgi5 == 0 and tt == 0) else []
                DMA("gpsimd", gld5[dl], g_s[trow:trow + 128, oc * 512:(oc + 1) * 512],
                    gx[dl], (s_wv1, max(0, s_wv1.n - 1)), *extra)
                DMA("gpsimd", xld5[dl], xsh[trow:trow + 128, oc * 512:(oc + 1) * 512], gx[dl])

                def f_wv(e, par=par, tt=tt, dl=dl,
                         w=W((s_wfr, s_wfr.n), (gx[dl], gx[dl].n),
                             (yst[dl], yst[dl].n))):
                    w(e)
                    e.tensor_tensor(t1s[dl], bank(par + tt), gld5[dl], OP.mult)
                    e.drain()
                    e.tensor_tensor(ost[dl], t1s[dl], xld5[dl], OP.add).then_inc(s_wv1.h, 1)
                em("vector", f_wv)
                s_wv1.n += 1
                DMA("gpsimd", y[trow:trow + 128, oc * 512:(oc + 1) * 512], ost[dl],
                    yst[dl], (s_wv1, s_wv1.n))

    def f_fin(e, w=W((yst[0], yst[0].n), (yst[1], yst[1].n))):
        w(e)
    em("gpsimd", f_fin)
    quiesce("sync", ldw)
    quiesce("gpsimd", gx + [s_lat])
    run_phase()
    return nc


def shard_inputs(c, L0_attn_hidden_states, L0_attn_Wqkv, L0_attn_q_ln_w, L0_attn_q_ln_b,
                 L0_attn_k_ln_w, L0_attn_k_ln_b, L0_attn_Wout, L0_ffn_base_w,
                 L0_ffn_base_b, L0_ffn_A, L0_ffn_B, L0_ffn_router_w):
    x = np.asarray(L0_attn_hidden_states, dtype=np.float32)
    com = {
        "wqkvT": np.ascontiguousarray(np.asarray(L0_attn_Wqkv, np.float32).T),
        "woutT": np.ascontiguousarray(np.asarray(L0_attn_Wout, np.float32).T),
        "basewT": np.ascontiguousarray(np.asarray(L0_ffn_base_w, np.float32).T),
        "baseb": np.asarray(L0_ffn_base_b, np.float32).reshape(1, c.D),
        "aT": np.ascontiguousarray(np.asarray(L0_ffn_A, np.float32).T),
        "routerT": np.ascontiguousarray(np.asarray(L0_ffn_router_w, np.float32).T),
        "bflat": np.ascontiguousarray(
            np.asarray(L0_ffn_B, np.float32).transpose(0, 2, 1).reshape(c.ER, c.D)),
        "qlnw": np.asarray(L0_attn_q_ln_w, np.float32).reshape(1, c.D),
        "qlnb": np.asarray(L0_attn_q_ln_b, np.float32).reshape(1, c.D),
        "klnw": np.asarray(L0_attn_k_ln_w, np.float32).reshape(1, c.D),
        "klnb": np.asarray(L0_attn_k_ln_b, np.float32).reshape(1, c.D),
    }
    in_maps = []
    for core in range(c.NC):
        b, r = core // c.R4, core % c.R4
        sh = dict(com)
        sh["xsh"] = np.ascontiguousarray(x[b, r * c.TOK:(r + 1) * c.TOK, :])
        sh["qpos"] = np.arange(r * c.TOK, (r + 1) * c.TOK, dtype=np.float32).reshape(c.TOK, 1)
        in_maps.append(sh)
    return in_maps


def gather_out(c, results):
    out = np.empty((c.B, c.S, c.D), np.float32)
    for core in range(c.NC):
        b, r = core // c.R4, core % c.R4
        out[b, r * c.TOK:(r + 1) * c.TOK, :] = results[core]["y"]
    return out


def kernel(**inputs):
    c = Cfg()
    in_maps = shard_inputs(c, **inputs)
    nc = build_program(c)
    res = run_bass_kernel_spmd(nc, in_maps, core_ids=list(range(c.NC)))
    return gather_out(c, res.results)
